# version 10
# baseline (speedup 1.0000x reference)
import numpy as np
import ml_dtypes
from contextlib import ExitStack

import concourse.bass as bass
import concourse.mybir as mybir
from concourse.bass_utils import run_bass_kernel_spmd

F32 = mybir.dt.float32
BF16 = mybir.dt.bfloat16
AF = mybir.ActivationFunctionType
OP = mybir.AluOpType
bfdt = ml_dtypes.bfloat16

D = 2048
HID = 8192
B = 2
S = 4096
T = 1024
DC = 16
HCN = 64
ROT, HALF = 1024, 512
C2 = float(np.float32(0.5 * (1.0 + 2.0 ** -9)))
EPS_LN = 1e-5
EPS_RMS = float(np.float32(np.finfo(np.float32).eps))
N_CORES = 8
QPRO_N = 24


def build(stage="full"):
    nc = bass.Bass("TRN2", target_bir_lowering=False)

    x_fm = nc.dram_tensor("x_fm", [DC, 128, T], F32, kind="ExternalInput").ap()
    wgu_sh = nc.dram_tensor("wgu_sh", [4, 4, 128, 16, 128], F32, kind="ExternalInput").ap()
    wp_sh = nc.dram_tensor("wp_sh", [2, 128, 64, 128], F32, kind="ExternalInput").ap()
    win_s = nc.dram_tensor("win_s", [48, 128, 16, 128], F32, kind="ExternalInput").ap()
    wout_s = nc.dram_tensor("wout_s", [16, 128, 16, 128], F32, kind="ExternalInput").ap()
    bin_c = nc.dram_tensor("bin_c", [128, 48], F32, kind="ExternalInput").ap()
    bout_c = nc.dram_tensor("bout_c", [128, 16], F32, kind="ExternalInput").ap()
    cost_d = nc.dram_tensor("cost_d", [128, 4 * T], F32, kind="ExternalInput").ap()
    sint_d = nc.dram_tensor("sint_d", [128, 4 * T], F32, kind="ExternalInput").ap()
    pblk_d = nc.dram_tensor("pblk_d", [128, 128], F32, kind="ExternalInput").ap()
    ones_d = nc.dram_tensor("ones_d", [128, 128], F32, kind="ExternalInput").ap()

    out_fm = nc.dram_tensor("out_fm", [DC, 128, T], F32, kind="ExternalOutput").ap()

    x2_dram = nc.dram_tensor("x2_dram", [DC, 128, T], BF16).ap()
    cache_p = nc.dram_tensor("cache_p", [16, 128, 8192], BF16)
    qsh_g = [nc.dram_tensor(f"qsh_g{p}", [4, 128, 2048], BF16) for p in range(4)]
    qsh_p = nc.dram_tensor("qsh_p", [2, 128, 8192], BF16)
    cache_q = [nc.dram_tensor(f"cache_q{p}", [32, 128, 2048], BF16) for p in range(4)]
    cc_in_h = nc.dram_tensor("cc_in", [128, 16], F32)
    cc_out_h = nc.dram_tensor("cc_out", [128, 16], F32)

    ctx = ExitStack()
    sem = {}
    cnt = {}

    def inc(instr, name, n=1):
        instr.then_inc(sem[name], n)
        cnt[name] += n

    def dmaw(eng, out, in_, q):
        eng.dma_start(out=out, in_=in_).then_inc(sem[q], 16)
        cnt[q] += 16
        return cnt[q]

    def W(eng, name, val):
        if val > 0:
            eng.wait_ge(sem[name], val)

    ENG2SEM = {"g": "dg", "sy": "ds", "s": "ss", "t": "ts", "v": "vs"}
    ENG_STEP = {"g": 16, "sy": 16, "s": 1, "t": 1, "v": 1}

    class Sched:

        def __init__(self):
            self.ops = {e: [] for e in ENG2SEM}
            self.tagpos = {}
            self.cum = {e: [] for e in ENG2SEM}
            self.counts = {}

        def add(self, eng, fn, tag=None, waits=(), sem_override=None):
            idx = len(self.ops[eng])
            sname, step = ((sem_override, 1) if sem_override
                           else (ENG2SEM[eng], ENG_STEP[eng]))
            self.counts[sname] = self.counts.get(sname, 0) + step
            if tag is not None:
                assert tag not in self.tagpos, tag
                self.tagpos[tag] = (sname, self.counts[sname])
            self.ops[eng].append((list(waits), fn, sname, step))

        def value(self, base, tag):
            sname, relcnt = self.tagpos[tag]
            return base[sname] + relcnt

        def emit(self, blk):
            base = {sn: cnt[sn] for sn in cnt}

            def body(eng_name, ops):
                def _(engine):
                    for waits, fn, sname, step in ops:
                        for w in waits:
                            if w is None:
                                continue
                            if isinstance(w, tuple):
                                wname, val = w
                            else:
                                wname = self.tagpos[w][0]
                                val = self.value(base, w)
                            if val > 0:
                                engine.wait_ge(sem[wname], val)
                        instr = fn(engine)
                        instr.then_inc(sem[sname], step)
                        cnt[sname] += step
                return _

            for eng_name, attr in (("g", "gpsimd"), ("sy", "sync"),
                                   ("s", "scalar"), ("t", "tensor"),
                                   ("v", "vector")):
                if self.ops[eng_name]:
                    getattr(blk, attr)(body(eng_name, self.ops[eng_name]))

    with ctx:
        for name in ("dg", "ds", "ts", "vs", "ss", "gs"):
            sem[name] = ctx.enter_context(nc.semaphore("sem_" + name))
            cnt[name] = 0
        sbt = lambda n, shp, dt: ctx.enter_context(nc.sbuf_tensor(n, shp, dt))
        pst = lambda n, shp: ctx.enter_context(nc.psum_tensor(n, shp, F32))

        XA = sbt("XA", [128, DC * T], BF16)
        BCM = sbt("BCM", [128, T], F32)
        BCI = sbt("BCI", [128, T], F32)
        BCV = sbt("BCV", [128, T], F32)
        SCB = sbt("SCB", [128, T], BF16)
        XST = sbt("XST", [128, T], F32)
        PB = sbt("PB", [128, 128], BF16)
        ON = sbt("ON", [128, 128], BF16)
        BIN = sbt("BIN", [128, 48], F32)
        BOUT = sbt("BOUT", [128, 16], F32)
        ACOL = sbt("ACOL", [128, 16], F32)
        AFULL = sbt("AFULL", [128, 16], F32)
        CST = sbt("CST", [128, 2], F32)

        ps_a = pst("ps_a", [128, T])
        ps_b = pst("ps_b", [128, T])
        ps_c = pst("ps_c", [128, T])
        ps_d = pst("ps_d", [128, T])

        AW_ = CP_ = CM_ = SSB_ = None
        QSLOT = None

        def xa(c):
            return XA[:, c * T:(c + 1) * T]

        def half(ap, h):
            return ap[:, h * 512:(h + 1) * 512]

        with nc.Block("consts") as blk:
            @blk.gpsimd
            def _(g):
                dmaw(g, PB[:, :], pblk_d, "dg")
                dmaw(g, ON[:, :], ones_d, "dg")
                dmaw(g, BIN[:, :], bin_c, "dg")
                dmaw(g, BOUT[:, :], bout_c, "dg")
                g.memset(CST[:, 0:1], EPS_LN)
                g.memset(CST[:, 1:2], EPS_RMS)
                W(g, "dg", cnt["dg"])

        def emit_ln(name, src_chunk, dst_chunk, stream_x=False):
            s0, t0, v0, d0 = cnt["ss"], cnt["ts"], cnt["vs"], cnt["ds"]
            nsq = 2 if stream_x else 1
            with nc.Block(name) as blk:
                if stream_x:
                    @blk.sync
                    def _(sy):
                        for c in range(DC):
                            if c >= 1:
                                W(sy, "ss", s0 + 2 * (c - 1) + 1)
                            dmaw(sy, XST[:, :], x_fm[c], "ds")

                @blk.scalar
                def _(sc):
                    for c in range(DC):
                        if stream_x:
                            W(sc, "ds", d0 + 16 * (c + 1))
                            i = sc.activation(dst_chunk(c), XST[:, :], AF.Identity)
                            inc(i, "ss")
                        if c >= 1:
                            W(sc, "ts", t0 + c)
                        src = dst_chunk(c) if stream_x else src_chunk(c)
                        i = sc.activation(SCB[:, :], src, AF.Square)
                        inc(i, "ss")

                @blk.tensor
                def _(t):
                    for c in range(DC):
                        W(t, "ss", s0 + nsq * c + nsq)
                        src = dst_chunk(c) if stream_x else src_chunk(c)
                        for h in range(2):
                            t.matmul(half(ps_a, h), ON[:, :], half(src, h),
                                     start=(c == 0), stop=(c == DC - 1))
                        for h in range(2):
                            i = t.matmul(half(ps_b, h), ON[:, :], half(SCB, h),
                                         start=(c == 0), stop=(c == DC - 1))
                        inc(i, "ts")

                @blk.vector
                def _(v):
                    W(v, "ts", t0 + DC)
                    i = v.tensor_single_scalar(BCM[:, :], ps_a[:, :], 1.0 / D,
                                               op=OP.mult)
                    inc(i, "vs")

                @blk.scalar
                def _(sc):
                    W(sc, "vs", v0 + 1)
                    i = sc.activation(BCV[:, :], BCM[:, :], AF.Square)
                    inc(i, "ss")

                @blk.vector
                def _(v):
                    W(v, "ss", s0 + DC * nsq + 1)
                    i = v.scalar_tensor_tensor(BCI[:, :], ps_b[:, :], 1.0 / D,
                                               BCV[:, :], op0=OP.mult,
                                               op1=OP.subtract)
                    inc(i, "vs")

                @blk.scalar
                def _(sc):
                    W(sc, "vs", v0 + 2)
                    i = sc.activation(BCV[:, :], BCI[:, :], AF.Sqrt, bias=CST[:, 0:1])
                    inc(i, "ss")

                @blk.vector
                def _(v):
                    W(v, "ss", s0 + DC * nsq + 2)
                    i = v.reciprocal(BCI[:, :], BCV[:, :])
                    inc(i, "vs")
                    m_bf = BCV[:, 0:512].bitcast(BF16)
                    i_bf = BCV[:, 512:1024].bitcast(BF16)
                    v.tensor_copy(m_bf, BCM[:, :])
                    i = v.tensor_copy(i_bf, BCI[:, :])
                    inc(i, "vs")
                    for c in range(DC):
                        src = dst_chunk(c) if stream_x else src_chunk(c)
                        v.tensor_tensor(SCB[:, :], src, m_bf, op=OP.subtract)
                        i = v.tensor_tensor(dst_chunk(c), SCB[:, :], i_bf,
                                            op=OP.mult)
                        inc(i, "vs")

        def mm_group(ps, slab_fn, mov_fn, nk, kc0=0, grp_start=True,
                     grp_stop=True):
            def fn(t):
                for k in range(nk):
                    for h in range(2):
                        i = t.matmul(half(ps, h), slab_fn(kc0 + k),
                                     half(mov_fn(kc0 + k), h),
                                     start=(grp_start and k == 0),
                                     stop=(grp_stop and k == nk - 1))
                return i
            return fn

        def q_abs2(slab, piece):
            def fn(sc):
                return sc.activation(AW_[:, :],
                                     slab[:, piece * T:(piece + 1) * T],
                                     AF.Abs)
            return fn

        def q_pb512(awoff, b):
            def fn(t):
                return t.matmul(half(ps_a, b), PB[:, :],
                                AW_[:, awoff * 512:(awoff + 1) * 512],
                                start=True, stop=True)
            return fn

        def q_vr512(slab, r, b):
            def fn(v):
                sl = slice(r * 512, (r + 1) * 512)
                ssl = slice((r % 2) * 512, (r % 2) * 512 + 512)
                v.tensor_copy(SSB_[:, ssl], half(ps_a, b))
                v.scalar_tensor_tensor(CP_[:, 0:512], SSB_[:, ssl], C2,
                                       slab[:, sl], op0=OP.mult,
                                       op1=OP.is_lt)
                v.scalar_tensor_tensor(CM_[:, 0:512], SSB_[:, ssl], -C2,
                                       slab[:, sl], op0=OP.mult,
                                       op1=OP.is_gt)
                v.scalar_tensor_tensor(CP_[:, 0:512], CP_[:, 0:512], 1.0,
                                       CM_[:, 0:512], op0=OP.bypass,
                                       op1=OP.subtract)
                return v.scalar_tensor_tensor(slab[:, sl], CP_[:, 0:512],
                                              1.0, SSB_[:, ssl],
                                              op0=OP.bypass, op1=OP.mult)
            return fn

        qbank = [0]
        prev_round = [None, None]

        def q_sched(sch, slab, r, vtag, abs_tag):
            b = qbank[0]
            qbank[0] ^= 1
            sch.add("t", q_pb512(r % 2, b), tag="pb_" + vtag,
                    waits=[abs_tag, prev_round[b]])
            sch.add("v", q_vr512(slab, r, b), tag=vtag,
                    waits=["pb_" + vtag])
            prev_round[b] = vtag

        QPIECES = ([("g", i // 4, i % 4) for i in range(16)]
                   + [("p", i // 4, i % 4) for i in range(8)])

        def q_src_of(pc):
            kind, a, b = pc
            if kind == "g":
                return wgu_sh[a][b]
            return wp_sh[a][:, b * 16:(b + 1) * 16, :]

        def q_dst_of(pc):
            kind, a, b = pc
            if kind == "g":
                return qsh_g[a].ap()[b]
            return qsh_p.ap()[a][:, b * 2048:(b + 1) * 2048]

        def q_piece(sch, i, cross_block):
            slot = QSLOT[0]
            pc = QPIECES[i]
            sch.add("g", (lambda: lambda g: g.dma_start(
                out=slot[:, :].rearrange("p (k o) -> p k o", o=128),
                in_=q_src_of(pc)))(), tag=f"qld{i}",
                waits=[None if cross_block else f"st{i-1}",
                       None if cross_block else f"v{i-1}r3"])
            sch.add("s", q_abs2(slot, 0), tag=f"ab{i}_0",
                    waits=[f"qld{i}",
                           None if cross_block else f"pb_v{i-1}r3"])
            q_sched(sch, slot, 0, f"v{i}r0", f"ab{i}_0")
            q_sched(sch, slot, 1, f"v{i}r1", f"ab{i}_0")
            sch.add("s", q_abs2(slot, 1), tag=f"ab{i}_1",
                    waits=[f"pb_v{i}r1"])
            q_sched(sch, slot, 2, f"v{i}r2", f"ab{i}_1")
            q_sched(sch, slot, 3, f"v{i}r3", f"ab{i}_1")
            sch.add("g", (lambda: lambda g: g.dma_start(
                out=q_dst_of(pc), in_=slot[:, :]))(), tag=f"st{i}",
                waits=[f"v{i}r3"])
            if pc == ("g", 0, 3) or pc[0] == "g" and pc[1] >= 1 and pc[2] == 3:
                q = pc[1]
                sch.add("g", (lambda q: lambda g: g.collective_compute(
                    "AllGather", OP.bypass,
                    replica_groups=[[0, 1, 2, 3, 4, 5, 6, 7]],
                    ins=[qsh_g[q].ap().opt()],
                    outs=[cache_q[q].ap().opt()]))(q), tag=f"ag{q}",
                    waits=[f"st{i}"], sem_override="gs")
            if i == len(QPIECES) - 1:
                sch.add("g", lambda g: g.collective_compute(
                    "AllGather", OP.bypass,
                    replica_groups=[[0, 1, 2, 3, 4, 5, 6, 7]],
                    ins=[qsh_p.ap().opt()],
                    outs=[cache_p.ap().opt()]), tag="agp",
                    waits=[f"st{i}"], sem_override="gs")

        def emit_quant_prologue(QB2):
            nonlocal AW_, CP_, CM_, SSB_, QSLOT
            QSLOT, AW_, SSB_ = QB2["QS"], QB2["AW"], QB2["SSB"]
            CM_ = BCM[:, 0:512].bitcast(BF16)
            CP_ = BCV[:, 0:512].bitcast(BF16)
            NPRO = QPRO_N
            sch = Sched()
            for i in range(NPRO):
                q_piece(sch, i, cross_block=(i == 0))
            with nc.Block("qpro") as blk:
                sch.emit(blk)
            QB2["npro"] = NPRO

        def emit_ffn_cached(name, xn_chunk, which, FB):
            HB, STG, SG = FB["HB"], FB["STG"], FB["SG"]
            RES = SCB
            quant = (which == 1)
            if quant:
                prev_round[0] = prev_round[1] = None

            def hb(c):
                return HB[:, c * T:(c + 1) * T]

            gsl = [STG[:, i * 2048:(i + 1) * 2048] for i in range(4)]

            def g_of(hc):
                return gsl[2 * (hc % 2)]

            def u_of(hc):
                return gsl[2 * (hc % 2) + 1]

            def gld(hc, up):
                m = hc % 16
                idx = 4 * (m // 2) + 2 * (m % 2) + (1 if up else 0)
                src = cache_q[hc // 16].ap()[idx]
                dstt = (u_of if up else g_of)(hc)

                def fn(g):
                    return g.dma_start(out=dstt, in_=src)
                return fn

            sch = Sched()
            for hc in (0, 1):
                sch.add("sy", gld(hc, False), tag=f"ldg{hc}",
                        waits=[("gs", gs_base[0] + hc // 16 + 1)
                               if which == 1 else None])
                sch.add("sy", gld(hc, True), tag=f"ldu{hc}")
            for hc in range(HCN - 2):
                sch.add("sy", gld(hc + 2, False), tag=f"ldg{hc+2}",
                        waits=[f"mmu{hc}",
                               ("gs", gs_base[0] + (hc + 2) // 16 + 1)
                               if which == 1 else None])
                sch.add("sy", gld(hc + 2, True), tag=f"ldu{hc+2}")
            for hc in range(HCN):
                npro = QB2["npro"]
                if quant and npro + hc < len(QPIECES):
                    q_piece(sch, npro + hc, cross_block=(hc == 0))
                sch.add("t", mm_group(ps_b, lambda k, hc=hc:
                                      g_of(hc)[:, k * 128:(k + 1) * 128],
                                      xn_chunk, DC),
                        tag=f"mmg{hc}",
                        waits=[f"ldg{hc}",
                               f"silu{hc-1}" if hc >= 1 else None])
                sch.add("s", (lambda hc: lambda sc: sc.activation(
                    SG[:, :], ps_b[:, :], AF.Silu))(hc), tag=f"silu{hc}",
                    waits=[f"mmg{hc}"])
                sch.add("t", mm_group(ps_c, lambda k, hc=hc:
                                      u_of(hc)[:, k * 128:(k + 1) * 128],
                                      xn_chunk, DC),
                        tag=f"mmu{hc}",
                        waits=[f"ldu{hc}",
                               f"h{hc-1}" if hc >= 1 else None])
                sch.add("v", (lambda hc: lambda v: v.tensor_tensor(
                    hb(hc), SG[:, :], ps_c[:, :], op=OP.mult))(hc),
                    tag=f"h{hc}", waits=[f"silu{hc}", f"mmu{hc}"])
                sch.add("s", (lambda hc: lambda sc: sc.activation(
                    SCB[:, :], hb(hc), AF.Square))(hc), tag=f"sq{hc}",
                    waits=[f"h{hc}", f"ssq{hc-1}" if hc >= 1 else None])
                sch.add("t", (lambda hc: lambda t: [
                    t.matmul(half(ps_d, 0), ON[:, :], half(SCB, 0),
                             start=(hc == 0), stop=(hc == HCN - 1)),
                    t.matmul(half(ps_d, 1), ON[:, :], half(SCB, 1),
                             start=(hc == 0), stop=(hc == HCN - 1))][-1])(hc),
                    tag=f"ssq{hc}", waits=[f"sq{hc}"])
            with nc.Block(name + "_gate") as blk:
                sch.emit(blk)

            sch = Sched()
            WPsl = STG[:, 0:8192]
            HLF = [STG[:, 0:4096], STG[:, 4096:8192]]
            dg_prev = cnt["dg"]
            for hh in range(2):
                sch.add("g", (lambda hh: lambda g: g.dma_start(
                    out=HLF[hh],
                    in_=cache_p.ap()[0][:, hh * 4096:(hh + 1) * 4096]))(hh),
                    tag=f"ldh{hh}_0",
                    waits=[("dg", dg_prev),
                           ("gs", gs_base[0] + 5) if which == 1 else None])
            for o in range(DC - 1):
                sch.add("g", (lambda o: lambda g: g.dma_start(
                    out=HLF[0], in_=cache_p.ap()[o + 1][:, 0:4096]))(o),
                    tag=f"ldh0_{o+1}", waits=[f"mmA{o}"])
                sch.add("g", (lambda o: lambda g: g.dma_start(
                    out=HLF[1], in_=cache_p.ap()[o + 1][:, 4096:8192]))(o),
                    tag=f"ldh1_{o+1}", waits=[f"mmB{o}"])
            sch.add("s", lambda sc: sc.activation(
                BCV[:, :], ps_d[:, :], AF.Sqrt, scale=1.0 / HID,
                bias=CST[:, 1:2]), tag="rmssd")
            sch.add("v", lambda v: v.reciprocal(BCI[:, :], BCV[:, :]),
                    tag="rmsinv", waits=["rmssd"])
            for o in range(DC):
                if which == 1:
                    sch.add("sy", (lambda o: lambda sy: sy.dma_start(
                        out=XST[:, :], in_=x_fm[o]))(o), tag=f"res{o}",
                        waits=[f"pevac{o-1}" if o >= 1 else None])
                else:
                    sch.add("sy", (lambda o: lambda sy: sy.dma_start(
                        out=RES[:, :], in_=x2_dram[o]))(o), tag=f"res{o}",
                        waits=[f"pevac{o-1}" if o >= 1 else None])
                    if o >= 1:
                        sch.add("sy", (lambda o: lambda sy: sy.dma_start(
                            out=out_fm[o - 1], in_=XST[:, :]))(o),
                            tag=f"out{o-1}", waits=[f"pevac{o-1}"])
            if which == 2:
                sch.add("sy", lambda sy: sy.dma_start(
                    out=out_fm[DC - 1], in_=XST[:, :]), tag=f"out{DC-1}",
                    waits=[f"pevac{DC-1}"])
            for o in range(DC):
                sch.add("t", mm_group(ps_b, lambda k:
                                      WPsl[:, k * 128:(k + 1) * 128],
                                      hb, 32, kc0=0, grp_stop=False),
                        tag=f"mmA{o}",
                        waits=[f"ldh0_{o}",
                               f"pp1_{o-1}" if o >= 1 else "rmsinv"])
                sch.add("t", mm_group(ps_b, lambda k:
                                      WPsl[:, k * 128:(k + 1) * 128],
                                      hb, 32, kc0=32, grp_start=False),
                        tag=f"mmB{o}", waits=[f"ldh1_{o}"])
                sch.add("v", lambda v: v.scalar_tensor_tensor(
                    BCV[:, :], ps_b[:, :], 0.5, BCI[:, :], op0=OP.mult,
                    op1=OP.mult), tag=f"pp1_{o}", waits=[f"mmB{o}"])
                if which == 1:
                    sch.add("v", (lambda o: lambda v: v.tensor_tensor(
                        xa(o), BCV[:, :], XST[:, :], op=OP.add))(o),
                        tag=f"pevac{o}", waits=[f"res{o}"])
                else:
                    sch.add("v", (lambda o: lambda v: v.tensor_tensor(
                        XST[:, :], BCV[:, :], RES[:, :], op=OP.add))(o),
                        tag=f"pevac{o}",
                        waits=[f"res{o}", f"out{o-2}" if o >= 2 else None])
            with nc.Block(name + "_proj") as blk:
                sch.emit(blk)
            if which == 2:
                with nc.Block(name + "_fin") as blk:
                    @blk.sync
                    def _(sy):
                        W(sy, "ds", cnt["ds"])

        def emit_qkv(AB):
            AB1, QB, KVB, COS, SIN = (AB["AB1"], AB["QB"], AB["KVB"],
                                      AB["COS"], AB["SIN"])
            WS = [AB["WS0"], AB["WS1"]]
            KS0, KS1, VS0 = AB["KS0"], AB["KS1"], AB["VS0"]
            RT1, RT2 = AB["RT1"], AB["RT2"]

            def ab1(c):
                return AB1[:, c * T:(c + 1) * T]

            def qb(c):
                return QB[:, c * T:(c + 1) * T]

            def kvb(c):
                return KVB[:, c * T:(c + 1) * T]

            def cosc(fc):
                return COS[:, fc * T:(fc + 1) * T]

            def sinc(fc):
                return SIN[:, fc * T:(fc + 1) * T]

            tasks = []
            for kp in range(4):
                tasks += [("k_e", kp), ("k_o", kp + 4), ("rope_k", kp),
                          ("v", kp), ("kv", kp), ("v", kp + 4), ("kv", kp + 4)]
            for c in range(8, 16):
                tasks += [("k_p", c), ("v", c), ("kv", c)]
            tasks += [("uprep", 0)]
            for qp in range(4):
                tasks += [("q", qp), ("q", qp + 4), ("rope_q", qp)]
            for c in range(8, 16):
                tasks += [("q", c)]

            MM_KINDS = ("q", "k_e", "k_o", "k_p", "v")
            g0, s0, t0, v0 = cnt["dg"], cnt["ss"], cnt["ts"], cnt["vs"]
            pss = [ps_a, ps_b]

            mm_tasks = [i for i, tk in enumerate(tasks) if tk[0] in MM_KINDS]
            mm_index = {i: n for n, i in enumerate(mm_tasks)}

            def out_chunk_of(tk):
                kind, idx = tk
                if kind == "q":
                    return idx
                if kind.startswith("k"):
                    return 16 + idx
                return 32 + idx

            def kbf_of(c):
                return KS1 if 4 <= c < 8 else KS0

            t_of, s_of, v_of = {}, {}, {}
            tn = sn = vn = 0
            last_kv = None
            last_kv_of_prev = {}
            v_accum_done = None
            for i, tk in enumerate(tasks):
                kind = tk[0]
                if kind in MM_KINDS:
                    tn += 1
                    t_of[i] = t0 + tn
                    sn += 1
                    s_of[i] = s0 + sn
                    last_kv_of_prev[i] = last_kv
                elif kind in ("rope_q", "rope_k"):
                    vn += 1
                    v_of[i] = v0 + vn
                    last_kv_of_prev[i] = last_kv
                elif kind == "kv":
                    vn += 1
                    v_of[i] = v0 + vn
                    sn += 1
                    s_of[i] = s0 + sn
                    tn += 1
                    t_of[i] = t0 + tn
                    last_kv_of_prev[i] = last_kv
                    last_kv = i
                elif kind == "uprep":
                    sn += 1
                    s_of[i] = s0 + sn
                    vn += 2 + DC
                    v_of[i] = v0 + vn
                    v_accum_done = v0 + vn
            t_total = tn
            load_done = {}

            with nc.Block("qkv") as blk:
                @blk.gpsimd
                def _(g):
                    dmaw(g, COS[:, :], cost_d, "dg")
                    dmaw(g, SIN[:, :], sint_d, "dg")
                    n_kv_mm = 32
                    for i in mm_tasks:
                        n = mm_index[i]
                        if n >= 2:
                            W(g, "ts", t_of[mm_tasks[n - 2]])
                        load_done[n] = dmaw(
                            g, WS[n % 2].rearrange("p (k o) -> p k o", o=128),
                            win_s[out_chunk_of(tasks[i])], "dg")
                        if n == n_kv_mm + 1:
                            W(g, "vs", v_accum_done)
                            dmaw(g, cc_in_h.ap(), ACOL[:, :], "dg")
                            W(g, "dg", cnt["dg"])
                            i2 = g.collective_compute(
                                "AllReduce", OP.add,
                                replica_groups=[[0, 1, 2, 3], [4, 5, 6, 7]],
                                ins=[cc_in_h.ap().opt()],
                                outs=[cc_out_h.ap().opt()])
                            inc(i2, "gs")
                    W(g, "gs", cnt["gs"])
                    dmaw(g, AFULL[:, :], cc_out_h.ap(), "dg")
                    W(g, "dg", cnt["dg"])

                @blk.tensor
                def _(t):
                    first_ksq = True
                    for i, tk in enumerate(tasks):
                        kind = tk[0]
                        if kind in MM_KINDS:
                            n = mm_index[i]
                            W(t, "dg", load_done[n])
                            if n >= 2:
                                W(t, "ss", s_of[mm_tasks[n - 2]])
                            ws = WS[n % 2]
                            psu = pss[n % 2]
                            for k in range(DC):
                                for h in range(2):
                                    i2 = t.matmul(half(psu, h),
                                                  ws[:, k * 128:(k + 1) * 128],
                                                  half(ab1(k), h),
                                                  start=(k == 0),
                                                  stop=(k == DC - 1))
                            inc(i2, "ts")
                        elif kind == "kv":
                            W(t, "ss", s_of[i])
                            for h in range(2):
                                i2 = t.matmul(half(ps_c, h), ON[:, :],
                                              half(SCB, h), start=first_ksq,
                                              stop=(tk[1] == 15))
                            first_ksq = False
                            inc(i2, "ts")

                @blk.scalar
                def _(sc):
                    for i, tk in enumerate(tasks):
                        kind = tk[0]
                        if kind in MM_KINDS:
                            n = mm_index[i]
                            oc = out_chunk_of(tk)
                            W(sc, "ts", t_of[i])
                            if kind == "q":
                                dst = qb(tk[1])
                            elif kind == "k_e":
                                if tk[1] >= 1:
                                    W(sc, "vs", v_of[i + 2 - 7])
                                dst = XST[:, :]
                            elif kind == "k_o":
                                dst = BCV[:, :]
                            elif kind == "k_p":
                                W(sc, "vs", v_of[last_kv_of_prev[i]])
                                dst = KS0[:, :]
                            else:
                                if last_kv_of_prev[i] is not None:
                                    W(sc, "vs", v_of[last_kv_of_prev[i]])
                                dst = VS0[:, :]
                            i2 = sc.activation(dst, pss[n % 2][:, :], AF.Identity,
                                               bias=BIN[:, oc:oc + 1])
                            inc(i2, "ss")
                        elif kind == "kv":
                            W(sc, "vs", v_of[i])
                            if last_kv_of_prev[i] is not None:
                                W(sc, "ts", t_of[last_kv_of_prev[i]])
                            i2 = sc.activation(SCB[:, :], kbf_of(tk[1])[:, :],
                                               AF.Square)
                            inc(i2, "ss")
                        elif kind == "uprep":
                            W(sc, "ts", t_of[last_kv])
                            i2 = sc.activation(RT1[:, :], ps_c[:, :], AF.Sqrt)
                            inc(i2, "ss")

                @blk.vector
                def _(v):
                    for i, tk in enumerate(tasks):
                        kind = tk[0]
                        if kind == "rope_q":
                            qp = tk[1]
                            W(v, "ss", s_of[i - 1])
                            v.tensor_tensor(RT1[:, :], qb(qp), cosc(qp), op=OP.mult)
                            v.tensor_tensor(RT2[:, :], qb(qp + 4), sinc(qp),
                                            op=OP.mult)
                            v.tensor_tensor(RT1[:, :], RT1[:, :], RT2[:, :],
                                            op=OP.subtract)
                            v.tensor_tensor(RT2[:, :], qb(qp + 4), cosc(qp),
                                            op=OP.mult)
                            v.tensor_tensor(BCM[:, :], qb(qp), sinc(qp), op=OP.mult)
                            v.tensor_tensor(qb(qp + 4), RT2[:, :], BCM[:, :],
                                            op=OP.add)
                            i2 = v.tensor_copy(qb(qp), RT1[:, :])
                            inc(i2, "vs")
                        elif kind == "rope_k":
                            kp = tk[1]
                            ko = BCV[:, :]
                            W(v, "ss", s_of[i - 1])
                            if last_kv_of_prev[i] is not None:
                                W(v, "ss", s_of[last_kv_of_prev[i]])
                            v.tensor_tensor(RT1[:, :], XST[:, :], cosc(kp),
                                            op=OP.mult)
                            v.tensor_tensor(RT2[:, :], ko, sinc(kp),
                                            op=OP.mult)
                            v.tensor_tensor(KS0[:, :], RT1[:, :], RT2[:, :],
                                            op=OP.subtract)
                            v.tensor_tensor(RT1[:, :], ko, cosc(kp),
                                            op=OP.mult)
                            v.tensor_tensor(RT2[:, :], XST[:, :], sinc(kp),
                                            op=OP.mult)
                            i2 = v.tensor_tensor(KS1[:, :], RT1[:, :], RT2[:, :],
                                                 op=OP.add)
                            inc(i2, "vs")
                        elif kind == "kv":
                            c = tk[1]
                            W(v, "ss", s_of[i - 1])
                            i2 = v.tensor_tensor(kvb(c), kbf_of(c)[:, :],
                                                 VS0[:, :], op=OP.mult)
                            inc(i2, "vs")
                        elif kind == "uprep":
                            W(v, "ss", s_of[i])
                            i2 = v.tensor_single_scalar(BCM[:, :], RT1[:, :],
                                                        1e-12, op=OP.max)
                            inc(i2, "vs")
                            i2 = v.reciprocal(BCI[:, :], BCM[:, :])
                            inc(i2, "vs")
                            W(v, "ts", t_of[last_kv])
                            for c in range(DC):
                                i2 = v.scalar_tensor_tensor(
                                    SCB[:, :], kvb(c), 1.0, BCI[:, :],
                                    op0=OP.bypass, op1=OP.mult,
                                    accum_out=ACOL[:, c:c + 1])
                                inc(i2, "vs")

        def emit_attn_out(AB):
            AB1, QB, KVB = AB["AB1"], AB["QB"], AB["KVB"]
            WS = [AB["WS0"], AB["WS1"]]

            def ab1(c):
                return AB1[:, c * T:(c + 1) * T]

            def qb(c):
                return QB[:, c * T:(c + 1) * T]

            def kvb(c):
                return KVB[:, c * T:(c + 1) * T]

            g0, s0, t0, v0 = cnt["dg"], cnt["ss"], cnt["ts"], cnt["vs"]
            with nc.Block("attnout") as blk:
                @blk.vector
                def _(v):
                    for c in range(DC):
                        i = v.tensor_single_scalar(kvb(c), qb(c),
                                                   AFULL[:, c:c + 1], op=OP.mult)
                        inc(i, "vs")

                @blk.gpsimd
                def _(g):
                    for o in range(DC):
                        if o >= 2:
                            W(g, "ts", t0 + o - 1)
                        dmaw(g, WS[o % 2].rearrange("p (k o) -> p k o", o=128),
                             wout_s[o], "dg")

                @blk.tensor
                def _(t):
                    for o in range(DC):
                        if o == 0:
                            W(t, "vs", v0 + DC)
                        W(t, "dg", g0 + 16 * (o + 1))
                        if o >= 2:
                            W(t, "ss", s0 + o - 1)
                        ws = WS[o % 2]
                        psu = [ps_a, ps_b][o % 2]
                        for k in range(DC):
                            for h in range(2):
                                i = t.matmul(half(psu, h),
                                             ws[:, k * 128:(k + 1) * 128],
                                             half(kvb(k), h),
                                             start=(k == 0), stop=(k == DC - 1))
                        inc(i, "ts")

                @blk.scalar
                def _(sc):
                    for o in range(DC):
                        W(sc, "ts", t0 + o + 1)
                        if o >= 1:
                            W(sc, "vs", v0 + DC + 2 * o)
                        psu = [ps_a, ps_b][o % 2]
                        i = sc.activation(XST[:, :], psu[:, :], AF.Identity,
                                          bias=BOUT[:, o:o + 1])
                        inc(i, "ss")

                @blk.vector
                def _(v):
                    for o in range(DC):
                        W(v, "ss", s0 + o + 1)
                        v.tensor_tensor(SCB[:, :], XST[:, :], ab1(o), op=OP.add)
                        i = v.tensor_tensor(qb(o), SCB[:, :], xa(o), op=OP.add)
                        inc(i, "vs", 2)

        def emit_spill_x2(AB):
            QB = AB["QB"]
            with nc.Block("spill") as blk:
                @blk.sync
                def _(sy):
                    for c in range(DC):
                        dmaw(sy, x2_dram[c], QB[:, c * T:(c + 1) * T], "ds")
                    W(sy, "ds", cnt["ds"])

        def emit_dump(src_chunk):
            with nc.Block("dump") as blk:
                @blk.gpsimd
                def _(g):
                    for c in range(DC):
                        dmaw(g, out_fm[c], src_chunk(c), "dg")
                    W(g, "dg", cnt["dg"])

        gs_base = [cnt["gs"]]
        with ExitStack() as qctx:
            qsb = lambda n, shp, dt: qctx.enter_context(nc.sbuf_tensor(n, shp, dt))
            QB2 = dict(
                QS=[qsb("QS0", [128, 2048], BF16)],
                AW=qsb("AW_q", [128, T], BF16),
                SSB=qsb("SSB_q", [128, T], BF16),
            )
            emit_quant_prologue(QB2)
            emit_ln("ln1", None, xa, stream_x=True)
            if stage == "xn1":
                emit_dump(xa)
                return nc

            with ExitStack() as fctx:
                fsb = lambda n, shp, dt: fctx.enter_context(nc.sbuf_tensor(n, shp, dt))
                FB = dict(
                    HB=fsb("HB", [128, HCN * T], BF16),
                    STG=fsb("STG", [128, 8192], BF16),
                    SG=fsb("SG_s", [128, T], F32),
                )
                emit_ffn_cached("ffn1", xa, 1, FB)
        with nc.Block("drain1") as blk:
            @blk.gpsimd
            def _(g):
                W(g, "dg", cnt["dg"])

            @blk.sync
            def _(sy):
                W(sy, "ds", cnt["ds"])
        if stage == "x1":
            emit_dump(xa)
            return nc

        with ExitStack() as actx:
            asb = lambda n, shp, dt: actx.enter_context(nc.sbuf_tensor(n, shp, dt))
            AB = dict(
                AB1=asb("AB1", [128, DC * T], BF16),
                QB=asb("QB", [128, DC * T], BF16),
                KVB=asb("KVB", [128, DC * T], BF16),
                COS=asb("COS", [128, 4 * T], F32),
                SIN=asb("SIN", [128, 4 * T], F32),
                WS0=asb("WS0", [128, 2048], BF16),
                WS1=asb("WS1", [128, 2048], BF16),
                KS0=asb("KS0", [128, T], BF16),
                KS1=asb("KS1", [128, T], BF16),
                VS0=asb("VS0", [128, T], BF16),
                RT1=asb("RT1", [128, T], F32),
                RT2=asb("RT2", [128, T], F32),
            )

            def ab1c(c):
                return AB["AB1"][:, c * T:(c + 1) * T]

            def qbc(c):
                return AB["QB"][:, c * T:(c + 1) * T]

            emit_ln("ln2", xa, ab1c)
            if stage == "xn2":
                emit_dump(ab1c)
                return nc
            emit_qkv(AB)
            if stage in ("q", "kv"):
                src = AB["QB"] if stage == "q" else AB["KVB"]
                emit_dump(lambda c: src[:, c * T:(c + 1) * T])
                return nc
            if stage == "afull":
                with nc.Block("adump") as blk:
                    @blk.gpsimd
                    def _(g):
                        dmaw(g, out_fm[0][:, 0:16], AFULL[:, :], "dg")
                        W(g, "dg", cnt["dg"])
                return nc
            emit_attn_out(AB)
            if stage == "x2":
                emit_dump(qbc)
                return nc
            emit_ln("ln3", qbc, xa)
            emit_spill_x2(AB)
            if stage == "xn3":
                emit_dump(xa)
                return nc

        with nc.Block("drain2") as blk:
            @blk.gpsimd
            def _(g):
                W(g, "dg", cnt["dg"])

            @blk.sync
            def _(sy):
                W(sy, "ds", cnt["ds"])
        with ExitStack() as fctx:
            fsb = lambda n, shp, dt: fctx.enter_context(nc.sbuf_tensor(n, shp, dt))
            FB = dict(
                HB=fsb("HB2", [128, HCN * T], BF16),
                STG=fsb("STG2", [128, 8192], BF16),
                SG=fsb("SG2", [128, T], F32),
            )
            emit_ffn_cached("ffn2", xa, 2, FB)

    return nc



def _perm():
    return np.concatenate([np.arange(0, ROT, 2), np.arange(1, ROT, 2),
                           np.arange(ROT, D)])


def _slab(w):
    o, i = w.shape
    return np.ascontiguousarray(
        w.reshape(o // 128, 128, i // 128, 128).transpose(0, 3, 2, 1))


def _prep_inputs(L0_attn_X, L0_attn_W_in, L0_attn_b_in, L0_attn_W_out,
                 L0_attn_b_out, L0_ffn_w_gate_up, L0_ffn_w_proj):
    p = _perm()
    f = lambda a: np.ascontiguousarray(np.asarray(a), dtype=np.float32)
    W_in = f(L0_attn_W_in)
    win_s = _slab(np.concatenate([W_in[:D][p], W_in[D:2 * D][p],
                                  W_in[2 * D:][p]], axis=0))
    b_in = f(L0_attn_b_in)
    b_perm = np.concatenate([b_in[:D][p], b_in[D:2 * D][p], b_in[2 * D:][p]])
    bin_c = np.ascontiguousarray(b_perm.reshape(48, 128).T)
    wout_s = _slab(f(L0_attn_W_out)[:, p])
    bout_c = np.ascontiguousarray(f(L0_attn_b_out).reshape(16, 128).T)
    wgu_slab = _slab(f(L0_ffn_w_gate_up))
    wp_slab = _slab(f(L0_ffn_w_proj))

    inv_freq = (1.0 / (10000.0 ** (np.arange(0, ROT, 2, dtype=np.float32)
                                   / np.float32(ROT)))).astype(np.float32)
    freqs = (np.arange(S, dtype=np.float32)[:, None]
             * inv_freq[None, :]).astype(np.float32)
    cosT = np.cos(freqs).astype(np.float32).T
    sinT = np.sin(freqs).astype(np.float32).T

    pblk = np.zeros((128, 128), np.float32)
    pblk[:64, :64] = 1.0 / 64
    pblk[64:, 64:] = 1.0 / 64
    ones = np.ones((128, 128), np.float32)

    shared = dict(win_s=win_s, wout_s=wout_s,
                  bin_c=bin_c, bout_c=bout_c, pblk_d=pblk, ones_d=ones)

    X = f(L0_attn_X)
    in_maps = []
    for core in range(N_CORES):
        b, q = core // 4, core % 4
        xs = X[b, q * T:(q + 1) * T]
        x_fm = np.ascontiguousarray(xs.T).reshape(DC, 128, T)
        cs = cosT[:, q * T:(q + 1) * T]
        sn = sinT[:, q * T:(q + 1) * T]
        cost = np.ascontiguousarray(
            cs.reshape(4, 128, T).transpose(1, 0, 2)).reshape(128, 4 * T)
        sint = np.ascontiguousarray(
            sn.reshape(4, 128, T).transpose(1, 0, 2)).reshape(128, 4 * T)
        idx = []
        for q in range(4):
            g0 = 16 * q + 2 * core
            idx += [g0, 64 + g0, g0 + 1, 64 + g0 + 1]
        wgu_sh = np.ascontiguousarray(wgu_slab[idx]).reshape(4, 4, 128, 16, 128)
        wp_sh = np.ascontiguousarray(wp_slab[2 * core:2 * core + 2])
        m = dict(shared)
        m.update(x_fm=x_fm, cost_d=cost, sint_d=sint,
                 wgu_sh=wgu_sh, wp_sh=wp_sh)
        in_maps.append(m)
    return in_maps


def kernel(L0_attn_X, L0_attn_W_in, L0_attn_b_in, L0_attn_W_out, L0_attn_b_out,
           L0_ffn_w_gate_up, L0_ffn_w_proj, _stage="full"):
    in_maps = _prep_inputs(L0_attn_X, L0_attn_W_in, L0_attn_b_in, L0_attn_W_out,
                           L0_attn_b_out, L0_ffn_w_gate_up, L0_ffn_w_proj)
    nc = build(stage=_stage)
    res = run_bass_kernel_spmd(nc, in_maps, core_ids=list(range(N_CORES)))
    out = np.empty((B, S, D), dtype=np.float32)
    for core in range(N_CORES):
        b, q = core // 4, core % 4
        o = np.asarray(res.results[core]["out_fm"], dtype=np.float32)
        out[b, q * T:(q + 1) * T] = o.reshape(D, T).T
    return out


# version 11
# speedup vs baseline: 1.0066x; 1.0066x over previous
import numpy as np
import ml_dtypes
from contextlib import ExitStack

import concourse.bass as bass
import concourse.mybir as mybir
from concourse.bass_utils import run_bass_kernel_spmd

F32 = mybir.dt.float32
BF16 = mybir.dt.bfloat16
AF = mybir.ActivationFunctionType
OP = mybir.AluOpType
bfdt = ml_dtypes.bfloat16

D = 2048
HID = 8192
B = 2
S = 4096
T = 1024
DC = 16
HCN = 64
ROT, HALF = 1024, 512
C2 = float(np.float32(0.5 * (1.0 + 2.0 ** -9)))
EPS_LN = 1e-5
EPS_RMS = float(np.float32(np.finfo(np.float32).eps))
N_CORES = 8


def build(stage="full"):
    nc = bass.Bass("TRN2", target_bir_lowering=False)

    x_fm = nc.dram_tensor("x_fm", [DC, 128, T], F32, kind="ExternalInput").ap()
    wgu_sh = nc.dram_tensor("wgu_sh", [4, 4, 128, 16, 128], F32, kind="ExternalInput").ap()
    wp_sh = nc.dram_tensor("wp_sh", [2, 128, 64, 128], F32, kind="ExternalInput").ap()
    win_s = nc.dram_tensor("win_s", [48, 128, 16, 128], F32, kind="ExternalInput").ap()
    wout_s = nc.dram_tensor("wout_s", [16, 128, 16, 128], F32, kind="ExternalInput").ap()
    bin_c = nc.dram_tensor("bin_c", [128, 48], F32, kind="ExternalInput").ap()
    bout_c = nc.dram_tensor("bout_c", [128, 16], F32, kind="ExternalInput").ap()
    cost_d = nc.dram_tensor("cost_d", [128, 4 * T], F32, kind="ExternalInput").ap()
    sint_d = nc.dram_tensor("sint_d", [128, 4 * T], F32, kind="ExternalInput").ap()
    pblk_d = nc.dram_tensor("pblk_d", [128, 128], F32, kind="ExternalInput").ap()
    ones_d = nc.dram_tensor("ones_d", [128, 128], F32, kind="ExternalInput").ap()

    out_fm = nc.dram_tensor("out_fm", [DC, 128, T], F32, kind="ExternalOutput").ap()

    x2_dram = nc.dram_tensor("x2_dram", [DC, 128, T], BF16).ap()
    cache_p = nc.dram_tensor("cache_p", [16, 128, 8192], BF16)
    qsh_g = [nc.dram_tensor(f"qsh_g{p}", [4, 128, 2048], BF16) for p in range(4)]
    qsh_p = nc.dram_tensor("qsh_p", [2, 128, 8192], BF16)
    cache_q = [nc.dram_tensor(f"cache_q{p}", [32, 128, 2048], BF16) for p in range(4)]
    cc_in_h = nc.dram_tensor("cc_in", [128, 16], F32)
    cc_out_h = nc.dram_tensor("cc_out", [128, 16], F32)

    ctx = ExitStack()
    sem = {}
    cnt = {}

    def inc(instr, name, n=1):
        instr.then_inc(sem[name], n)
        cnt[name] += n

    def dmaw(eng, out, in_, q):
        eng.dma_start(out=out, in_=in_).then_inc(sem[q], 16)
        cnt[q] += 16
        return cnt[q]

    def W(eng, name, val):
        if val > 0:
            eng.wait_ge(sem[name], val)

    ENG2SEM = {"g": "dg", "sy": "ds", "s": "ss", "t": "ts", "v": "vs"}
    ENG_STEP = {"g": 16, "sy": 16, "s": 1, "t": 1, "v": 1}

    class Sched:

        def __init__(self):
            self.ops = {e: [] for e in ENG2SEM}
            self.tagpos = {}
            self.cum = {e: [] for e in ENG2SEM}
            self.counts = {}

        def add(self, eng, fn, tag=None, waits=(), sem_override=None):
            idx = len(self.ops[eng])
            sname, step = ((sem_override, 1) if sem_override
                           else (ENG2SEM[eng], ENG_STEP[eng]))
            self.counts[sname] = self.counts.get(sname, 0) + step
            if tag is not None:
                assert tag not in self.tagpos, tag
                self.tagpos[tag] = (sname, self.counts[sname])
            self.ops[eng].append((list(waits), fn, sname, step))

        def value(self, base, tag):
            sname, relcnt = self.tagpos[tag]
            return base[sname] + relcnt

        def emit(self, blk):
            base = {sn: cnt[sn] for sn in cnt}

            def body(eng_name, ops):
                def _(engine):
                    for waits, fn, sname, step in ops:
                        for w in waits:
                            if w is None:
                                continue
                            if isinstance(w, tuple):
                                wname, val = w
                            else:
                                wname = self.tagpos[w][0]
                                val = self.value(base, w)
                            if val > 0:
                                engine.wait_ge(sem[wname], val)
                        instr = fn(engine)
                        instr.then_inc(sem[sname], step)
                        cnt[sname] += step
                return _

            for eng_name, attr in (("g", "gpsimd"), ("sy", "sync"),
                                   ("s", "scalar"), ("t", "tensor"),
                                   ("v", "vector")):
                if self.ops[eng_name]:
                    getattr(blk, attr)(body(eng_name, self.ops[eng_name]))

    with ctx:
        for name in ("dg", "ds", "ts", "vs", "ss", "gs"):
            sem[name] = ctx.enter_context(nc.semaphore("sem_" + name))
            cnt[name] = 0
        sbt = lambda n, shp, dt: ctx.enter_context(nc.sbuf_tensor(n, shp, dt))
        pst = lambda n, shp: ctx.enter_context(nc.psum_tensor(n, shp, F32))

        XA = sbt("XA", [128, DC * T], BF16)
        BCM = sbt("BCM", [128, T], F32)
        BCI = sbt("BCI", [128, T], F32)
        BCV = sbt("BCV", [128, T], F32)
        SCB = sbt("SCB", [128, T], BF16)
        XST = sbt("XST", [128, T], F32)
        PB = sbt("PB", [128, 128], BF16)
        ON = sbt("ON", [128, 128], BF16)
        BIN = sbt("BIN", [128, 48], F32)
        BOUT = sbt("BOUT", [128, 16], F32)
        ACOL = sbt("ACOL", [128, 16], F32)
        AFULL = sbt("AFULL", [128, 16], F32)
        CST = sbt("CST", [128, 2], F32)

        ps_a = pst("ps_a", [128, T])
        ps_b = pst("ps_b", [128, T])
        ps_c = pst("ps_c", [128, T])
        ps_d = pst("ps_d", [128, T])

        AW_ = CP_ = CM_ = SSB_ = None
        QSLOT = None

        def xa(c):
            return XA[:, c * T:(c + 1) * T]

        def half(ap, h):
            return ap[:, h * 512:(h + 1) * 512]

        with nc.Block("consts") as blk:
            @blk.gpsimd
            def _(g):
                dmaw(g, PB[:, :], pblk_d, "dg")
                dmaw(g, ON[:, :], ones_d, "dg")
                dmaw(g, BIN[:, :], bin_c, "dg")
                dmaw(g, BOUT[:, :], bout_c, "dg")
                g.memset(CST[:, 0:1], EPS_LN)
                g.memset(CST[:, 1:2], EPS_RMS)
                W(g, "dg", cnt["dg"])

        def emit_ln(name, src_chunk, dst_chunk, stream_x=False):
            s0, t0, v0, d0 = cnt["ss"], cnt["ts"], cnt["vs"], cnt["ds"]
            nsq = 2 if stream_x else 1
            with nc.Block(name) as blk:
                if stream_x:
                    @blk.sync
                    def _(sy):
                        for c in range(DC):
                            if c >= 1:
                                W(sy, "ss", s0 + 2 * (c - 1) + 1)
                            dmaw(sy, XST[:, :], x_fm[c], "ds")

                @blk.scalar
                def _(sc):
                    for c in range(DC):
                        if stream_x:
                            W(sc, "ds", d0 + 16 * (c + 1))
                            i = sc.activation(dst_chunk(c), XST[:, :], AF.Identity)
                            inc(i, "ss")
                        if c >= 1:
                            W(sc, "ts", t0 + c)
                        src = dst_chunk(c) if stream_x else src_chunk(c)
                        i = sc.activation(SCB[:, :], src, AF.Square)
                        inc(i, "ss")

                @blk.tensor
                def _(t):
                    for c in range(DC):
                        W(t, "ss", s0 + nsq * c + nsq)
                        src = dst_chunk(c) if stream_x else src_chunk(c)
                        for h in range(2):
                            t.matmul(half(ps_a, h), ON[:, :], half(src, h),
                                     start=(c == 0), stop=(c == DC - 1))
                        for h in range(2):
                            i = t.matmul(half(ps_b, h), ON[:, :], half(SCB, h),
                                         start=(c == 0), stop=(c == DC - 1))
                        inc(i, "ts")

                @blk.vector
                def _(v):
                    W(v, "ts", t0 + DC)
                    i = v.tensor_single_scalar(BCM[:, :], ps_a[:, :], 1.0 / D,
                                               op=OP.mult)
                    inc(i, "vs")

                @blk.scalar
                def _(sc):
                    W(sc, "vs", v0 + 1)
                    i = sc.activation(BCV[:, :], BCM[:, :], AF.Square)
                    inc(i, "ss")

                @blk.vector
                def _(v):
                    W(v, "ss", s0 + DC * nsq + 1)
                    i = v.scalar_tensor_tensor(BCI[:, :], ps_b[:, :], 1.0 / D,
                                               BCV[:, :], op0=OP.mult,
                                               op1=OP.subtract)
                    inc(i, "vs")

                @blk.scalar
                def _(sc):
                    W(sc, "vs", v0 + 2)
                    i = sc.activation(BCV[:, :], BCI[:, :], AF.Sqrt, bias=CST[:, 0:1])
                    inc(i, "ss")

                @blk.vector
                def _(v):
                    W(v, "ss", s0 + DC * nsq + 2)
                    i = v.reciprocal(BCI[:, :], BCV[:, :])
                    inc(i, "vs")
                    m_bf = BCV[:, 0:512].bitcast(BF16)
                    i_bf = BCV[:, 512:1024].bitcast(BF16)
                    v.tensor_copy(m_bf, BCM[:, :])
                    i = v.tensor_copy(i_bf, BCI[:, :])
                    inc(i, "vs")
                    for c in range(DC):
                        src = dst_chunk(c) if stream_x else src_chunk(c)
                        v.tensor_tensor(SCB[:, :], src, m_bf, op=OP.subtract)
                        i = v.tensor_tensor(dst_chunk(c), SCB[:, :], i_bf,
                                            op=OP.mult)
                        inc(i, "vs")

        def mm_group(ps, slab_fn, mov_fn, nk, kc0=0, grp_start=True,
                     grp_stop=True):
            def fn(t):
                for k in range(nk):
                    for h in range(2):
                        i = t.matmul(half(ps, h), slab_fn(kc0 + k),
                                     half(mov_fn(kc0 + k), h),
                                     start=(grp_start and k == 0),
                                     stop=(grp_stop and k == nk - 1))
                return i
            return fn

        def q_abs2(slab, piece):
            def fn(sc):
                return sc.activation(AW_[:, :],
                                     slab[:, piece * T:(piece + 1) * T],
                                     AF.Abs)
            return fn

        def q_pb512(awoff, b):
            def fn(t):
                return t.matmul(half(ps_a, b), PB[:, :],
                                AW_[:, awoff * 512:(awoff + 1) * 512],
                                start=True, stop=True)
            return fn

        def q_vr512(slab, r, b):
            def fn(v):
                sl = slice(r * 512, (r + 1) * 512)
                ssl = slice((r % 2) * 512, (r % 2) * 512 + 512)
                v.tensor_copy(SSB_[:, ssl], half(ps_a, b))
                v.scalar_tensor_tensor(CP_[:, 0:512], SSB_[:, ssl], C2,
                                       slab[:, sl], op0=OP.mult,
                                       op1=OP.is_lt)
                v.scalar_tensor_tensor(CM_[:, 0:512], SSB_[:, ssl], -C2,
                                       slab[:, sl], op0=OP.mult,
                                       op1=OP.is_gt)
                v.scalar_tensor_tensor(CP_[:, 0:512], CP_[:, 0:512], 1.0,
                                       CM_[:, 0:512], op0=OP.bypass,
                                       op1=OP.subtract)
                return v.scalar_tensor_tensor(slab[:, sl], CP_[:, 0:512],
                                              1.0, SSB_[:, ssl],
                                              op0=OP.bypass, op1=OP.mult)
            return fn

        qbank = [0]
        prev_round = [None, None]

        def q_sched(sch, slab, r, vtag, abs_tag):
            b = qbank[0]
            qbank[0] ^= 1
            sch.add("t", q_pb512(r % 2, b), tag="pb_" + vtag,
                    waits=[abs_tag, prev_round[b]])
            sch.add("v", q_vr512(slab, r, b), tag=vtag,
                    waits=["pb_" + vtag])
            prev_round[b] = vtag

        QPIECES = ([("g", i // 4, i % 4) for i in range(16)]
                   + [("p", i // 4, i % 4) for i in range(8)])

        def q_src_of(pc):
            kind, a, b = pc
            if kind == "g":
                return wgu_sh[a][b]
            return wp_sh[a][:, b * 16:(b + 1) * 16, :]

        def q_dst_of(pc):
            kind, a, b = pc
            if kind == "g":
                return qsh_g[a].ap()[b]
            return qsh_p.ap()[a][:, b * 2048:(b + 1) * 2048]

        def q_piece(sch, i):
            slot = QSLOT[i % 2]
            pc = QPIECES[i]
            sch.add("g", (lambda slot: lambda g: g.dma_start(
                out=slot[:, :].rearrange("p (k o) -> p k o", o=128),
                in_=q_src_of(pc)))(slot), tag=f"qld{i}",
                waits=[f"st{i-2}" if i >= 2 else None,
                       f"v{i-2}r3" if i >= 2 else None])
            sch.add("s", q_abs2(slot, 0), tag=f"ab{i}_0",
                    waits=[f"qld{i}",
                           f"pb_v{i-1}r3" if i >= 1 else None])
            q_sched(sch, slot, 0, f"v{i}r0", f"ab{i}_0")
            q_sched(sch, slot, 1, f"v{i}r1", f"ab{i}_0")
            sch.add("s", q_abs2(slot, 1), tag=f"ab{i}_1",
                    waits=[f"pb_v{i}r1"])
            q_sched(sch, slot, 2, f"v{i}r2", f"ab{i}_1")
            q_sched(sch, slot, 3, f"v{i}r3", f"ab{i}_1")
            sch.add("g", (lambda slot: lambda g: g.dma_start(
                out=q_dst_of(pc), in_=slot[:, :]))(slot), tag=f"st{i}",
                waits=[f"v{i}r3"])
            if pc == ("g", 0, 3) or pc[0] == "g" and pc[1] >= 1 and pc[2] == 3:
                q = pc[1]
                sch.add("g", (lambda q: lambda g: g.collective_compute(
                    "AllGather", OP.bypass,
                    replica_groups=[[0, 1, 2, 3, 4, 5, 6, 7]],
                    ins=[qsh_g[q].ap().opt()],
                    outs=[cache_q[q].ap().opt()]))(q), tag=f"ag{q}",
                    waits=[f"st{i}"], sem_override="gs")
            if i == len(QPIECES) - 1:
                sch.add("g", lambda g: g.collective_compute(
                    "AllGather", OP.bypass,
                    replica_groups=[[0, 1, 2, 3, 4, 5, 6, 7]],
                    ins=[qsh_p.ap().opt()],
                    outs=[cache_p.ap().opt()]), tag="agp",
                    waits=[f"st{i}"], sem_override="gs")

        def emit_quant_prologue(QB2):
            nonlocal AW_, CP_, CM_, SSB_, QSLOT
            QSLOT, AW_, SSB_ = QB2["QS"], QB2["AW"], QB2["SSB"]
            CM_ = BCM[:, 0:512].bitcast(BF16)
            CP_ = BCV[:, 0:512].bitcast(BF16)
            sch = Sched()
            for i in range(len(QPIECES)):
                q_piece(sch, i)
            with nc.Block("qpro") as blk:
                sch.emit(blk)

        def emit_ffn_cached(name, xn_chunk, which, FB):
            HB, STG, SG = FB["HB"], FB["STG"], FB["SG"]
            RES = SCB
            quant = (which == 1)
            if quant:
                prev_round[0] = prev_round[1] = None

            def hb(c):
                return HB[:, c * T:(c + 1) * T]

            gsl = [STG[:, i * 2048:(i + 1) * 2048] for i in range(4)]

            def g_of(hc):
                return gsl[2 * (hc % 2)]

            def u_of(hc):
                return gsl[2 * (hc % 2) + 1]

            def gld(hc, up):
                m = hc % 16
                idx = 4 * (m // 2) + 2 * (m % 2) + (1 if up else 0)
                src = cache_q[hc // 16].ap()[idx]
                dstt = (u_of if up else g_of)(hc)

                def fn(g):
                    return g.dma_start(out=dstt, in_=src)
                return fn

            sch = Sched()
            for hc in (0, 1):
                sch.add("sy", gld(hc, False), tag=f"ldg{hc}",
                        waits=[("gs", gs_base[0] + hc // 16 + 1)
                               if which == 1 else None])
                sch.add("sy", gld(hc, True), tag=f"ldu{hc}")
            for hc in range(HCN - 2):
                sch.add("sy", gld(hc + 2, False), tag=f"ldg{hc+2}",
                        waits=[f"mmu{hc}",
                               ("gs", gs_base[0] + (hc + 2) // 16 + 1)
                               if which == 1 else None])
                sch.add("sy", gld(hc + 2, True), tag=f"ldu{hc+2}")
            for hc in range(HCN):
                sch.add("t", mm_group(ps_b, lambda k, hc=hc:
                                      g_of(hc)[:, k * 128:(k + 1) * 128],
                                      xn_chunk, DC),
                        tag=f"mmg{hc}",
                        waits=[f"ldg{hc}",
                               f"silu{hc-1}" if hc >= 1 else None])
                sch.add("s", (lambda hc: lambda sc: sc.activation(
                    SG[:, :], ps_b[:, :], AF.Silu))(hc), tag=f"silu{hc}",
                    waits=[f"mmg{hc}"])
                sch.add("t", mm_group(ps_c, lambda k, hc=hc:
                                      u_of(hc)[:, k * 128:(k + 1) * 128],
                                      xn_chunk, DC),
                        tag=f"mmu{hc}",
                        waits=[f"ldu{hc}",
                               f"h{hc-1}" if hc >= 1 else None])
                sch.add("v", (lambda hc: lambda v: v.tensor_tensor(
                    hb(hc), SG[:, :], ps_c[:, :], op=OP.mult))(hc),
                    tag=f"h{hc}", waits=[f"silu{hc}", f"mmu{hc}"])
                sch.add("s", (lambda hc: lambda sc: sc.activation(
                    SCB[:, :], hb(hc), AF.Square))(hc), tag=f"sq{hc}",
                    waits=[f"h{hc}", f"ssq{hc-1}" if hc >= 1 else None])
                sch.add("t", (lambda hc: lambda t: [
                    t.matmul(half(ps_d, 0), ON[:, :], half(SCB, 0),
                             start=(hc == 0), stop=(hc == HCN - 1)),
                    t.matmul(half(ps_d, 1), ON[:, :], half(SCB, 1),
                             start=(hc == 0), stop=(hc == HCN - 1))][-1])(hc),
                    tag=f"ssq{hc}", waits=[f"sq{hc}"])
            with nc.Block(name + "_gate") as blk:
                sch.emit(blk)

            sch = Sched()
            WPsl = STG[:, 0:8192]
            HLF = [STG[:, 0:4096], STG[:, 4096:8192]]
            dg_prev = cnt["dg"]
            for hh in range(2):
                sch.add("g", (lambda hh: lambda g: g.dma_start(
                    out=HLF[hh],
                    in_=cache_p.ap()[0][:, hh * 4096:(hh + 1) * 4096]))(hh),
                    tag=f"ldh{hh}_0",
                    waits=[("dg", dg_prev),
                           ("gs", gs_base[0] + 5) if which == 1 else None])
            for o in range(DC - 1):
                sch.add("g", (lambda o: lambda g: g.dma_start(
                    out=HLF[0], in_=cache_p.ap()[o + 1][:, 0:4096]))(o),
                    tag=f"ldh0_{o+1}", waits=[f"mmA{o}"])
                sch.add("g", (lambda o: lambda g: g.dma_start(
                    out=HLF[1], in_=cache_p.ap()[o + 1][:, 4096:8192]))(o),
                    tag=f"ldh1_{o+1}", waits=[f"mmB{o}"])
            sch.add("s", lambda sc: sc.activation(
                BCV[:, :], ps_d[:, :], AF.Sqrt, scale=1.0 / HID,
                bias=CST[:, 1:2]), tag="rmssd")
            sch.add("v", lambda v: v.reciprocal(BCI[:, :], BCV[:, :]),
                    tag="rmsinv", waits=["rmssd"])
            for o in range(DC):
                if which == 1:
                    sch.add("sy", (lambda o: lambda sy: sy.dma_start(
                        out=XST[:, :], in_=x_fm[o]))(o), tag=f"res{o}",
                        waits=[f"pevac{o-1}" if o >= 1 else None])
                else:
                    sch.add("sy", (lambda o: lambda sy: sy.dma_start(
                        out=RES[:, :], in_=x2_dram[o]))(o), tag=f"res{o}",
                        waits=[f"pevac{o-1}" if o >= 1 else None])
                    if o >= 1:
                        sch.add("sy", (lambda o: lambda sy: sy.dma_start(
                            out=out_fm[o - 1], in_=XST[:, :]))(o),
                            tag=f"out{o-1}", waits=[f"pevac{o-1}"])
            if which == 2:
                sch.add("sy", lambda sy: sy.dma_start(
                    out=out_fm[DC - 1], in_=XST[:, :]), tag=f"out{DC-1}",
                    waits=[f"pevac{DC-1}"])
            for o in range(DC):
                sch.add("t", mm_group(ps_b, lambda k:
                                      WPsl[:, k * 128:(k + 1) * 128],
                                      hb, 32, kc0=0, grp_stop=False),
                        tag=f"mmA{o}",
                        waits=[f"ldh0_{o}",
                               f"pp1_{o-1}" if o >= 1 else "rmsinv"])
                sch.add("t", mm_group(ps_b, lambda k:
                                      WPsl[:, k * 128:(k + 1) * 128],
                                      hb, 32, kc0=32, grp_start=False),
                        tag=f"mmB{o}", waits=[f"ldh1_{o}"])
                sch.add("v", lambda v: v.scalar_tensor_tensor(
                    BCV[:, :], ps_b[:, :], 0.5, BCI[:, :], op0=OP.mult,
                    op1=OP.mult), tag=f"pp1_{o}", waits=[f"mmB{o}"])
                if which == 1:
                    sch.add("v", (lambda o: lambda v: v.tensor_tensor(
                        xa(o), BCV[:, :], XST[:, :], op=OP.add))(o),
                        tag=f"pevac{o}", waits=[f"res{o}"])
                else:
                    sch.add("v", (lambda o: lambda v: v.tensor_tensor(
                        XST[:, :], BCV[:, :], RES[:, :], op=OP.add))(o),
                        tag=f"pevac{o}",
                        waits=[f"res{o}", f"out{o-2}" if o >= 2 else None])
            with nc.Block(name + "_proj") as blk:
                sch.emit(blk)
            if which == 2:
                with nc.Block(name + "_fin") as blk:
                    @blk.sync
                    def _(sy):
                        W(sy, "ds", cnt["ds"])

        def emit_qkv(AB):
            AB1, QB, KVB, COS, SIN = (AB["AB1"], AB["QB"], AB["KVB"],
                                      AB["COS"], AB["SIN"])
            WS = [AB["WS0"], AB["WS1"]]
            KS0, KS1, VS0 = AB["KS0"], AB["KS1"], AB["VS0"]
            RT1, RT2 = AB["RT1"], AB["RT2"]

            def ab1(c):
                return AB1[:, c * T:(c + 1) * T]

            def qb(c):
                return QB[:, c * T:(c + 1) * T]

            def kvb(c):
                return KVB[:, c * T:(c + 1) * T]

            def cosc(fc):
                return COS[:, fc * T:(fc + 1) * T]

            def sinc(fc):
                return SIN[:, fc * T:(fc + 1) * T]

            tasks = []
            for kp in range(4):
                tasks += [("k_e", kp), ("k_o", kp + 4), ("rope_k", kp),
                          ("v", kp), ("kv", kp), ("v", kp + 4), ("kv", kp + 4)]
            for c in range(8, 16):
                tasks += [("k_p", c), ("v", c), ("kv", c)]
            tasks += [("uprep", 0)]
            for qp in range(4):
                tasks += [("q", qp), ("q", qp + 4), ("rope_q", qp)]
            for c in range(8, 16):
                tasks += [("q", c)]

            MM_KINDS = ("q", "k_e", "k_o", "k_p", "v")
            g0, s0, t0, v0 = cnt["dg"], cnt["ss"], cnt["ts"], cnt["vs"]
            pss = [ps_a, ps_b]

            mm_tasks = [i for i, tk in enumerate(tasks) if tk[0] in MM_KINDS]
            mm_index = {i: n for n, i in enumerate(mm_tasks)}

            def out_chunk_of(tk):
                kind, idx = tk
                if kind == "q":
                    return idx
                if kind.startswith("k"):
                    return 16 + idx
                return 32 + idx

            def kbf_of(c):
                return KS1 if 4 <= c < 8 else KS0

            t_of, s_of, v_of = {}, {}, {}
            tn = sn = vn = 0
            last_kv = None
            last_kv_of_prev = {}
            v_accum_done = None
            for i, tk in enumerate(tasks):
                kind = tk[0]
                if kind in MM_KINDS:
                    tn += 1
                    t_of[i] = t0 + tn
                    sn += 1
                    s_of[i] = s0 + sn
                    last_kv_of_prev[i] = last_kv
                elif kind in ("rope_q", "rope_k"):
                    vn += 1
                    v_of[i] = v0 + vn
                    last_kv_of_prev[i] = last_kv
                elif kind == "kv":
                    vn += 1
                    v_of[i] = v0 + vn
                    sn += 1
                    s_of[i] = s0 + sn
                    tn += 1
                    t_of[i] = t0 + tn
                    last_kv_of_prev[i] = last_kv
                    last_kv = i
                elif kind == "uprep":
                    sn += 1
                    s_of[i] = s0 + sn
                    vn += 2 + DC
                    v_of[i] = v0 + vn
                    v_accum_done = v0 + vn
            t_total = tn
            load_done = {}

            with nc.Block("qkv") as blk:
                @blk.gpsimd
                def _(g):
                    dmaw(g, COS[:, :], cost_d, "dg")
                    dmaw(g, SIN[:, :], sint_d, "dg")
                    n_kv_mm = 32
                    for i in mm_tasks:
                        n = mm_index[i]
                        if n >= 2:
                            W(g, "ts", t_of[mm_tasks[n - 2]])
                        load_done[n] = dmaw(
                            g, WS[n % 2].rearrange("p (k o) -> p k o", o=128),
                            win_s[out_chunk_of(tasks[i])], "dg")
                        if n == n_kv_mm + 1:
                            W(g, "vs", v_accum_done)
                            dmaw(g, cc_in_h.ap(), ACOL[:, :], "dg")
                            W(g, "dg", cnt["dg"])
                            i2 = g.collective_compute(
                                "AllReduce", OP.add,
                                replica_groups=[[0, 1, 2, 3], [4, 5, 6, 7]],
                                ins=[cc_in_h.ap().opt()],
                                outs=[cc_out_h.ap().opt()])
                            inc(i2, "gs")
                    W(g, "gs", cnt["gs"])
                    dmaw(g, AFULL[:, :], cc_out_h.ap(), "dg")
                    W(g, "dg", cnt["dg"])

                @blk.tensor
                def _(t):
                    first_ksq = True
                    for i, tk in enumerate(tasks):
                        kind = tk[0]
                        if kind in MM_KINDS:
                            n = mm_index[i]
                            W(t, "dg", load_done[n])
                            if n >= 2:
                                W(t, "ss", s_of[mm_tasks[n - 2]])
                            ws = WS[n % 2]
                            psu = pss[n % 2]
                            for k in range(DC):
                                for h in range(2):
                                    i2 = t.matmul(half(psu, h),
                                                  ws[:, k * 128:(k + 1) * 128],
                                                  half(ab1(k), h),
                                                  start=(k == 0),
                                                  stop=(k == DC - 1))
                            inc(i2, "ts")
                        elif kind == "kv":
                            W(t, "ss", s_of[i])
                            for h in range(2):
                                i2 = t.matmul(half(ps_c, h), ON[:, :],
                                              half(SCB, h), start=first_ksq,
                                              stop=(tk[1] == 15))
                            first_ksq = False
                            inc(i2, "ts")

                @blk.scalar
                def _(sc):
                    for i, tk in enumerate(tasks):
                        kind = tk[0]
                        if kind in MM_KINDS:
                            n = mm_index[i]
                            oc = out_chunk_of(tk)
                            W(sc, "ts", t_of[i])
                            if kind == "q":
                                dst = qb(tk[1])
                            elif kind == "k_e":
                                if tk[1] >= 1:
                                    W(sc, "vs", v_of[i + 2 - 7])
                                dst = XST[:, :]
                            elif kind == "k_o":
                                dst = BCV[:, :]
                            elif kind == "k_p":
                                W(sc, "vs", v_of[last_kv_of_prev[i]])
                                dst = KS0[:, :]
                            else:
                                if last_kv_of_prev[i] is not None:
                                    W(sc, "vs", v_of[last_kv_of_prev[i]])
                                dst = VS0[:, :]
                            i2 = sc.activation(dst, pss[n % 2][:, :], AF.Identity,
                                               bias=BIN[:, oc:oc + 1])
                            inc(i2, "ss")
                        elif kind == "kv":
                            W(sc, "vs", v_of[i])
                            if last_kv_of_prev[i] is not None:
                                W(sc, "ts", t_of[last_kv_of_prev[i]])
                            i2 = sc.activation(SCB[:, :], kbf_of(tk[1])[:, :],
                                               AF.Square)
                            inc(i2, "ss")
                        elif kind == "uprep":
                            W(sc, "ts", t_of[last_kv])
                            i2 = sc.activation(RT1[:, :], ps_c[:, :], AF.Sqrt)
                            inc(i2, "ss")

                @blk.vector
                def _(v):
                    for i, tk in enumerate(tasks):
                        kind = tk[0]
                        if kind == "rope_q":
                            qp = tk[1]
                            W(v, "ss", s_of[i - 1])
                            v.tensor_tensor(RT1[:, :], qb(qp), cosc(qp), op=OP.mult)
                            v.tensor_tensor(RT2[:, :], qb(qp + 4), sinc(qp),
                                            op=OP.mult)
                            v.tensor_tensor(RT1[:, :], RT1[:, :], RT2[:, :],
                                            op=OP.subtract)
                            v.tensor_tensor(RT2[:, :], qb(qp + 4), cosc(qp),
                                            op=OP.mult)
                            v.tensor_tensor(BCM[:, :], qb(qp), sinc(qp), op=OP.mult)
                            v.tensor_tensor(qb(qp + 4), RT2[:, :], BCM[:, :],
                                            op=OP.add)
                            i2 = v.tensor_copy(qb(qp), RT1[:, :])
                            inc(i2, "vs")
                        elif kind == "rope_k":
                            kp = tk[1]
                            ko = BCV[:, :]
                            W(v, "ss", s_of[i - 1])
                            if last_kv_of_prev[i] is not None:
                                W(v, "ss", s_of[last_kv_of_prev[i]])
                            v.tensor_tensor(RT1[:, :], XST[:, :], cosc(kp),
                                            op=OP.mult)
                            v.tensor_tensor(RT2[:, :], ko, sinc(kp),
                                            op=OP.mult)
                            v.tensor_tensor(KS0[:, :], RT1[:, :], RT2[:, :],
                                            op=OP.subtract)
                            v.tensor_tensor(RT1[:, :], ko, cosc(kp),
                                            op=OP.mult)
                            v.tensor_tensor(RT2[:, :], XST[:, :], sinc(kp),
                                            op=OP.mult)
                            i2 = v.tensor_tensor(KS1[:, :], RT1[:, :], RT2[:, :],
                                                 op=OP.add)
                            inc(i2, "vs")
                        elif kind == "kv":
                            c = tk[1]
                            W(v, "ss", s_of[i - 1])
                            i2 = v.tensor_tensor(kvb(c), kbf_of(c)[:, :],
                                                 VS0[:, :], op=OP.mult)
                            inc(i2, "vs")
                        elif kind == "uprep":
                            W(v, "ss", s_of[i])
                            i2 = v.tensor_single_scalar(BCM[:, :], RT1[:, :],
                                                        1e-12, op=OP.max)
                            inc(i2, "vs")
                            i2 = v.reciprocal(BCI[:, :], BCM[:, :])
                            inc(i2, "vs")
                            W(v, "ts", t_of[last_kv])
                            for c in range(DC):
                                i2 = v.scalar_tensor_tensor(
                                    SCB[:, :], kvb(c), 1.0, BCI[:, :],
                                    op0=OP.bypass, op1=OP.mult,
                                    accum_out=ACOL[:, c:c + 1])
                                inc(i2, "vs")

        def emit_attn_out(AB):
            AB1, QB, KVB = AB["AB1"], AB["QB"], AB["KVB"]
            WS = [AB["WS0"], AB["WS1"]]

            def ab1(c):
                return AB1[:, c * T:(c + 1) * T]

            def qb(c):
                return QB[:, c * T:(c + 1) * T]

            def kvb(c):
                return KVB[:, c * T:(c + 1) * T]

            g0, s0, t0, v0 = cnt["dg"], cnt["ss"], cnt["ts"], cnt["vs"]
            with nc.Block("attnout") as blk:
                @blk.vector
                def _(v):
                    for c in range(DC):
                        i = v.tensor_single_scalar(kvb(c), qb(c),
                                                   AFULL[:, c:c + 1], op=OP.mult)
                        inc(i, "vs")

                @blk.gpsimd
                def _(g):
                    for o in range(DC):
                        if o >= 2:
                            W(g, "ts", t0 + o - 1)
                        dmaw(g, WS[o % 2].rearrange("p (k o) -> p k o", o=128),
                             wout_s[o], "dg")

                @blk.tensor
                def _(t):
                    for o in range(DC):
                        if o == 0:
                            W(t, "vs", v0 + DC)
                        W(t, "dg", g0 + 16 * (o + 1))
                        if o >= 2:
                            W(t, "ss", s0 + o - 1)
                        ws = WS[o % 2]
                        psu = [ps_a, ps_b][o % 2]
                        for k in range(DC):
                            for h in range(2):
                                i = t.matmul(half(psu, h),
                                             ws[:, k * 128:(k + 1) * 128],
                                             half(kvb(k), h),
                                             start=(k == 0), stop=(k == DC - 1))
                        inc(i, "ts")

                @blk.scalar
                def _(sc):
                    for o in range(DC):
                        W(sc, "ts", t0 + o + 1)
                        if o >= 1:
                            W(sc, "vs", v0 + DC + 2 * o)
                        psu = [ps_a, ps_b][o % 2]
                        i = sc.activation(XST[:, :], psu[:, :], AF.Identity,
                                          bias=BOUT[:, o:o + 1])
                        inc(i, "ss")

                @blk.vector
                def _(v):
                    for o in range(DC):
                        W(v, "ss", s0 + o + 1)
                        v.tensor_tensor(SCB[:, :], XST[:, :], ab1(o), op=OP.add)
                        i = v.tensor_tensor(qb(o), SCB[:, :], xa(o), op=OP.add)
                        inc(i, "vs", 2)

        def emit_spill_x2(AB):
            QB = AB["QB"]
            with nc.Block("spill") as blk:
                @blk.sync
                def _(sy):
                    for c in range(DC):
                        dmaw(sy, x2_dram[c], QB[:, c * T:(c + 1) * T], "ds")
                    W(sy, "ds", cnt["ds"])

        def emit_dump(src_chunk):
            with nc.Block("dump") as blk:
                @blk.gpsimd
                def _(g):
                    for c in range(DC):
                        dmaw(g, out_fm[c], src_chunk(c), "dg")
                    W(g, "dg", cnt["dg"])

        gs_base = [cnt["gs"]]
        with ExitStack() as qctx:
            qsb = lambda n, shp, dt: qctx.enter_context(nc.sbuf_tensor(n, shp, dt))
            QB2 = dict(
                QS=[qsb("QS0", [128, 2048], BF16), qsb("QS1", [128, 2048], BF16)],
                AW=qsb("AW_q", [128, T], BF16),
                SSB=qsb("SSB_q", [128, T], BF16),
            )
            emit_quant_prologue(QB2)
        emit_ln("ln1", None, xa, stream_x=True)
        if stage == "xn1":
            emit_dump(xa)
            return nc

        with ExitStack() as fctx:
            fsb = lambda n, shp, dt: fctx.enter_context(nc.sbuf_tensor(n, shp, dt))
            FB = dict(
                HB=fsb("HB", [128, HCN * T], BF16),
                STG=fsb("STG", [128, 8192], BF16),
                SG=fsb("SG_s", [128, T], F32),
            )
            emit_ffn_cached("ffn1", xa, 1, FB)
        with nc.Block("drain1") as blk:
            @blk.gpsimd
            def _(g):
                W(g, "dg", cnt["dg"])

            @blk.sync
            def _(sy):
                W(sy, "ds", cnt["ds"])
        if stage == "x1":
            emit_dump(xa)
            return nc

        with ExitStack() as actx:
            asb = lambda n, shp, dt: actx.enter_context(nc.sbuf_tensor(n, shp, dt))
            AB = dict(
                AB1=asb("AB1", [128, DC * T], BF16),
                QB=asb("QB", [128, DC * T], BF16),
                KVB=asb("KVB", [128, DC * T], BF16),
                COS=asb("COS", [128, 4 * T], F32),
                SIN=asb("SIN", [128, 4 * T], F32),
                WS0=asb("WS0", [128, 2048], BF16),
                WS1=asb("WS1", [128, 2048], BF16),
                KS0=asb("KS0", [128, T], BF16),
                KS1=asb("KS1", [128, T], BF16),
                VS0=asb("VS0", [128, T], BF16),
                RT1=asb("RT1", [128, T], F32),
                RT2=asb("RT2", [128, T], F32),
            )

            def ab1c(c):
                return AB["AB1"][:, c * T:(c + 1) * T]

            def qbc(c):
                return AB["QB"][:, c * T:(c + 1) * T]

            emit_ln("ln2", xa, ab1c)
            if stage == "xn2":
                emit_dump(ab1c)
                return nc
            emit_qkv(AB)
            if stage in ("q", "kv"):
                src = AB["QB"] if stage == "q" else AB["KVB"]
                emit_dump(lambda c: src[:, c * T:(c + 1) * T])
                return nc
            if stage == "afull":
                with nc.Block("adump") as blk:
                    @blk.gpsimd
                    def _(g):
                        dmaw(g, out_fm[0][:, 0:16], AFULL[:, :], "dg")
                        W(g, "dg", cnt["dg"])
                return nc
            emit_attn_out(AB)
            if stage == "x2":
                emit_dump(qbc)
                return nc
            emit_ln("ln3", qbc, xa)
            emit_spill_x2(AB)
            if stage == "xn3":
                emit_dump(xa)
                return nc

        with nc.Block("drain2") as blk:
            @blk.gpsimd
            def _(g):
                W(g, "dg", cnt["dg"])

            @blk.sync
            def _(sy):
                W(sy, "ds", cnt["ds"])
        with ExitStack() as fctx:
            fsb = lambda n, shp, dt: fctx.enter_context(nc.sbuf_tensor(n, shp, dt))
            FB = dict(
                HB=fsb("HB2", [128, HCN * T], BF16),
                STG=fsb("STG2", [128, 8192], BF16),
                SG=fsb("SG2", [128, T], F32),
            )
            emit_ffn_cached("ffn2", xa, 2, FB)

    return nc



def _perm():
    return np.concatenate([np.arange(0, ROT, 2), np.arange(1, ROT, 2),
                           np.arange(ROT, D)])


def _slab(w):
    o, i = w.shape
    return np.ascontiguousarray(
        w.reshape(o // 128, 128, i // 128, 128).transpose(0, 3, 2, 1))


def _prep_inputs(L0_attn_X, L0_attn_W_in, L0_attn_b_in, L0_attn_W_out,
                 L0_attn_b_out, L0_ffn_w_gate_up, L0_ffn_w_proj):
    p = _perm()
    f = lambda a: np.ascontiguousarray(np.asarray(a), dtype=np.float32)
    W_in = f(L0_attn_W_in)
    win_s = _slab(np.concatenate([W_in[:D][p], W_in[D:2 * D][p],
                                  W_in[2 * D:][p]], axis=0))
    b_in = f(L0_attn_b_in)
    b_perm = np.concatenate([b_in[:D][p], b_in[D:2 * D][p], b_in[2 * D:][p]])
    bin_c = np.ascontiguousarray(b_perm.reshape(48, 128).T)
    wout_s = _slab(f(L0_attn_W_out)[:, p])
    bout_c = np.ascontiguousarray(f(L0_attn_b_out).reshape(16, 128).T)
    wgu_slab = _slab(f(L0_ffn_w_gate_up))
    wp_slab = _slab(f(L0_ffn_w_proj))

    inv_freq = (1.0 / (10000.0 ** (np.arange(0, ROT, 2, dtype=np.float32)
                                   / np.float32(ROT)))).astype(np.float32)
    freqs = (np.arange(S, dtype=np.float32)[:, None]
             * inv_freq[None, :]).astype(np.float32)
    cosT = np.cos(freqs).astype(np.float32).T
    sinT = np.sin(freqs).astype(np.float32).T

    pblk = np.zeros((128, 128), np.float32)
    pblk[:64, :64] = 1.0 / 64
    pblk[64:, 64:] = 1.0 / 64
    ones = np.ones((128, 128), np.float32)

    shared = dict(win_s=win_s, wout_s=wout_s,
                  bin_c=bin_c, bout_c=bout_c, pblk_d=pblk, ones_d=ones)

    X = f(L0_attn_X)
    in_maps = []
    for core in range(N_CORES):
        b, q = core // 4, core % 4
        xs = X[b, q * T:(q + 1) * T]
        x_fm = np.ascontiguousarray(xs.T).reshape(DC, 128, T)
        cs = cosT[:, q * T:(q + 1) * T]
        sn = sinT[:, q * T:(q + 1) * T]
        cost = np.ascontiguousarray(
            cs.reshape(4, 128, T).transpose(1, 0, 2)).reshape(128, 4 * T)
        sint = np.ascontiguousarray(
            sn.reshape(4, 128, T).transpose(1, 0, 2)).reshape(128, 4 * T)
        idx = []
        for q in range(4):
            g0 = 16 * q + 2 * core
            idx += [g0, 64 + g0, g0 + 1, 64 + g0 + 1]
        wgu_sh = np.ascontiguousarray(wgu_slab[idx]).reshape(4, 4, 128, 16, 128)
        wp_sh = np.ascontiguousarray(wp_slab[2 * core:2 * core + 2])
        m = dict(shared)
        m.update(x_fm=x_fm, cost_d=cost, sint_d=sint,
                 wgu_sh=wgu_sh, wp_sh=wp_sh)
        in_maps.append(m)
    return in_maps


def kernel(L0_attn_X, L0_attn_W_in, L0_attn_b_in, L0_attn_W_out, L0_attn_b_out,
           L0_ffn_w_gate_up, L0_ffn_w_proj, _stage="full"):
    in_maps = _prep_inputs(L0_attn_X, L0_attn_W_in, L0_attn_b_in, L0_attn_W_out,
                           L0_attn_b_out, L0_ffn_w_gate_up, L0_ffn_w_proj)
    nc = build(stage=_stage)
    res = run_bass_kernel_spmd(nc, in_maps, core_ids=list(range(N_CORES)))
    out = np.empty((B, S, D), dtype=np.float32)
    for core in range(N_CORES):
        b, q = core // 4, core % 4
        o = np.asarray(res.results[core]["out_fm"], dtype=np.float32)
        out[b, q * T:(q + 1) * T] = o.reshape(D, T).T
    return out
